# version 04 (final)
# baseline (speedup 1.0000x reference)
import numpy as np
import ml_dtypes
import concourse.bass as bass
import concourse.mybir as mybir
from concourse.bass_utils import run_bass_kernel_spmd
from contextlib import ExitStack

F32 = mybir.dt.float32
BF16 = mybir.dt.bfloat16
AF = mybir.ActivationFunctionType
ALU = mybir.AluOpType
AX = mybir.AxisListType
BF = ml_dtypes.bfloat16

B, T, D = 2, 8192, 2048
H, DK, DV, M = 16, 128, 128, 64
FD, E = 1408, 8
NCORES = 8
SPAN = (B * T) // NCORES
NCH = SPAN // 128
ND = D // 128
EPS = 1e-5
CLAMP = 32.0
ROPE_BASE = 10000.0
SCALE = DK ** (-0.5)
DROWS = 193
NFT = FD // 128


class Sem:
    __slots__ = ("h", "n")

    def __init__(self, h):
        self.h = h
        self.n = 0


class Grp:
    __slots__ = ("fresh",)

    def __init__(self):
        self.fresh = True


class Plan:
    def __init__(self):
        self.ops = []

    def op(self, eng, fn, waits=(), incs=()):
        ws = [(s, t) for (s, t) in waits if s is not None and t]
        self.ops.append((eng, ws, fn, list(incs)))
        last = None
        for s, k in incs:
            s.n += k
            last = s.n
        return last

    def emit(self, block):
        hw = {}

        def run(eng_obj, name):
            for engine, waits, fn, incs in self.ops:
                if engine != name:
                    continue
                for s, t in waits:
                    key = id(s)
                    if hw.get((name, key), 0) >= t:
                        continue
                    eng_obj.wait_ge(s.h, t)
                    hw[(name, key)] = t
                inst = fn(eng_obj)
                for s, k in incs:
                    inst.then_inc(s.h, k)

        block.gpsimd(lambda g: run(g, "gpsimd"))
        block.sync(lambda s: run(s, "sync"))
        block.tensor(lambda t: run(t, "tensor"))
        block.vector(lambda v: run(v, "vector"))
        block.scalar(lambda s: run(s, "scalar"))


def build_nc(upto=None, fulldbg=False):
    nc = bass.Bass("TRN2", target_bir_lowering=False)
    P = Plan()

    def din(name, shape, dt=F32):
        return nc.dram_tensor(name, shape, dt, kind="ExternalInput")

    x_in = din("x", [SPAN, D])
    wqT = din("wqT", [D, D]); wkT = din("wkT", [D, D])
    wvT = din("wvT", [D, D]); wgT = din("wgT", [D, D])
    woT = din("woT", [D, D])
    ws1T = din("ws1T", [D, 16]); ws2T = din("ws2T", [16, H * M])
    bs2 = din("bs2", [1, H * M])
    gatewT = din("gatewT", [D, E])
    gateT = din("gateT", [E * D, FD]); upT = din("upT", [E * D, FD])
    downT = din("downT", [E * FD, D])
    cosq_in = din("cosq", [64, SPAN]); sinq_in = din("sinq", [64, SPAN])
    cosk_in = din("cosk", [64, SPAN]); sink_in = din("sink", [64, SPAN])
    ident_in = din("identb", [128, 128], BF16)
    identf_in = din("identf", [128, 128])
    umaskb_in = din("umaskb", [128, 128], BF16)
    umask16_in = din("umask16", [128, H * 128])
    ones_f_in = din("onesf", [1, 128])
    onescol_in = din("onescol", [128, 1], BF16)
    gnw_in = din("gnwrep", [128, H * DV])
    coef_in = din("coef", [128, NCORES])
    y_out = nc.dram_tensor("y", [SPAN, D], F32, kind="ExternalOutput")

    def dsc(name, shape, dt):
        return nc.dram_tensor(name, shape, dt)

    wqT_b = dsc("wqT_b", [D, D], BF16); wkT_b = dsc("wkT_b", [D, D], BF16)
    wvT_b = dsc("wvT_b", [D, D], BF16); wgT_b = dsc("wgT_b", [D, D], BF16)
    woT_b = dsc("woT_b", [D, D], BF16)
    ws1T_b = dsc("ws1T_b", [D, 16], BF16)
    ws2T_b = dsc("ws2T_b", [16, H * M], BF16)
    gatewT_b = dsc("gatewT_b", [D, E], BF16)
    gateT_b = dsc("gateT_b", [E * D, FD], BF16)
    upT_b = dsc("upT_b", [E * D, FD], BF16)
    downT_b = dsc("downT_b", [E * FD, D], BF16)

    xnT_s = dsc("xnT_s", [D, SPAN], BF16)
    qT_s = dsc("qT_s", [NCH, 128, H, 128], BF16)
    kT_s = dsc("kT_s", [NCH, 128, H, 128], BF16)
    ktm_s = dsc("ktm_s", [NCH, 128, H * 128], BF16)
    vtm_s = dsc("vtm_s", [NCH, 128, H * 128], BF16)
    sgtm_s = dsc("sgtm_s", [NCH, 128, H * 128], BF16)
    estm_s = dsc("estm_s", [NCH, 128, H * 64], BF16)
    zcs_s = dsc("zcs_s", [NCH, H * M], F32)
    twT_s = dsc("twT_s", [E, SPAN], F32)
    ffn_s = dsc("ffn_s", [SPAN, D], F32)
    dIn = dsc("dIn", [DROWS, 2048], F32)
    dOut = dsc("dOut", [NCORES * DROWS, 2048], F32)

    top = ExitStack()

    def sem(name):
        return Sem(top.enter_context(nc.semaphore(name)))

    cGw = sem("cGw"); cW1 = sem("cW1"); cW2s = sem("cW2s")
    cWk = sem("cWk"); cWv = sem("cWv"); cWq = sem("cWq"); cWg = sem("cWg")
    cGate = sem("cGate"); cUp = sem("cUp"); cDown = sem("cDown"); cWo = sem("cWo")
    dConst = sem("dConst")
    dW = sem("dW")
    dX = [sem("dX0"), sem("dX1")]
    dXn = sem("dXn")
    dEs = sem("dEs"); dZw = sem("dZw")
    dKw = [sem("dKw0"), sem("dKw1")]
    dKtm = [sem(f"dKtm{i}") for i in range(4)]
    dVw = [sem("dVw0"), sem("dVw1")]
    dGw = [sem("dGw0"), sem("dGw1")]
    dDel = sem("dDel"); ccS = sem("ccS")
    dGat = [sem("dGat0"), sem("dGat1")]
    dDl = [sem("dDl0"), sem("dDl1")]
    dXnq = sem("dXnq")
    dTww = sem("dTww")
    dTw = [sem("dTw0"), sem("dTw1")]
    dSg = [sem("dSg0"), sem("dSg1")]
    dSd = [sem("dSd0"), sem("dSd1")]
    dFo = [sem("dFo0"), sem("dFo1")]
    dP4 = [sem("dP40"), sem("dP41")]
    dP5 = [sem("dP50"), sem("dP51")]
    dYo = [sem("dYo0"), sem("dYo1")]
    tS = sem("tS"); vS = sem("vS"); sS = sem("sS")
    ALL_SEMS = [dConst, dW, dX[0], dX[1], dXn, dEs, dZw, dKw[0], dKw[1],
                dKtm[0], dKtm[1], dKtm[2], dKtm[3], dVw[0], dVw[1],
                dGw[0], dGw[1], dDel, dGat[0], dGat[1], dDl[0], dDl[1],
                dXnq, dTww, dTw[0], dTw[1], dSg[0], dSg[1], dSd[0], dSd[1],
                dFo[0], dFo[1], dP4[0], dP4[1], dP5[0], dP5[1], dYo[0],
                dYo[1], tS, vS, sS]

    def barrier():
        return [(s, s.n) for s in ALL_SEMS]

    def sb(name, shape, dt):
        return nc.alloc_sbuf_tensor(name, shape, dt)

    ident_sb = sb("ident_sb", [128, 128], BF16)
    identf_sb = sb("identf_sb", [128, 128], F32)
    umaskb_sb = sb("umaskb_sb", [128, 128], BF16)
    ones_f_sb = sb("ones_f_sb", [1, 128], F32)
    onescol_sb = sb("onescol_sb", [128, 1], BF16)
    coef_sb = sb("coef_sb", [128, NCORES], F32)
    eps_sb = sb("eps_sb", [128, 1], F32)
    S1sb = sb("S1sb", [128, H * M], F32)
    S2sb = sb("S2sb", [64, H * DV], F32)
    Zin_sb = sb("Zin_sb", [1, H * M], F32)

    def dma(queue, out, in_, csem, waits=(), k=16):
        return P.op(queue, lambda e, o=out, i=in_: e.dma_start(out=o, in_=i),
                    waits, [(csem, k)])

    def mm(out_ap, lhsT, rhs, g, stop, waits=()):
        st = g.fresh
        g.fresh = False
        return P.op("tensor",
                    lambda e, o=out_ap, l=lhsT, r=rhs, s=st, p=stop:
                    e.matmul(o, l, r, start=s, stop=p),
                    waits, [(tS, 1)])

    def tr(out_ap, in_ap, idt, waits=()):
        return P.op("tensor",
                    lambda e, o=out_ap, i=in_ap, d=idt: e.transpose(o, i, d),
                    waits, [(tS, 1)])

    def v_op(fn, waits=()):
        return P.op("vector", fn, waits, [(vS, 1)])

    def s_op(fn, waits=()):
        return P.op("scalar", fn, waits, [(sS, 1)])


    def _dbg_finish(dumps):
        dsem = sem("dbgd")
        for nm, t, waits in dumps:
            shape = list(t.shape)
            o = nc.dram_tensor("dbg_" + nm, shape, t.dtype,
                               kind="ExternalOutput")
            P.op("gpsimd", lambda e, oo=o, tt_=t: e.dma_start(
                out=oo.ap(), in_=tt_.ap()), waits, [(dsem, 16)])
        P.op("gpsimd", lambda e: e.dma_start(
            out=dIn[0:1, 0:4], in_=x_in[0:1, 0:4]),
            [(dsem, dsem.n)], [(dsem, 16)])
        with nc.Block() as block:
            P.emit(block)
        top.close()
        return nc

    for csem, dst, src in [(cGw, gatewT_b, gatewT), (cW1, ws1T_b, ws1T),
                           (cW2s, ws2T_b, ws2T), (cWk, wkT_b, wkT),
                           (cWv, wvT_b, wvT), (cWq, wqT_b, wqT),
                           (cWg, wgT_b, wgT)]:
        dma("gpsimd", dst[:, :], src[:, :], csem)
    for e_ in range(E):
        dma("gpsimd", gateT_b[e_ * D:(e_ + 1) * D, :],
            gateT[e_ * D:(e_ + 1) * D, :], cGate)
    for e_ in range(E):
        dma("gpsimd", upT_b[e_ * D:(e_ + 1) * D, :],
            upT[e_ * D:(e_ + 1) * D, :], cUp)
    for e_ in range(E):
        dma("gpsimd", downT_b[e_ * FD:(e_ + 1) * FD, :],
            downT[e_ * FD:(e_ + 1) * FD, :], cDown)
    dma("gpsimd", woT_b[:, :], woT[:, :], cWo)

    for dst, src in [(ident_sb, ident_in), (identf_sb, identf_in),
                     (umaskb_sb, umaskb_in),
                     (ones_f_sb, ones_f_in), (onescol_sb, onescol_in),
                     (coef_sb, coef_in)]:
        dma("sync", dst[:, :], src[:, :], dConst)
    eps_set = v_op(lambda e: e.memset(eps_sb[:, :], float(EPS)))

    p1 = ExitStack()

    def sb1(name, shape, dt):
        return p1.enter_context(nc.sbuf_tensor(name, shape, dt))

    xnT_sb = sb1("xnT_sb", [128, ND * SPAN], BF16)
    zrow_stage = sb1("zrow_stage", [1, H * M], F32)
    p1rs = ExitStack()
    ws1T_sb = p1rs.enter_context(nc.sbuf_tensor("ws1T_sb", [128, ND * 16], BF16))
    ws2T_sb = p1rs.enter_context(nc.sbuf_tensor("ws2T_sb", [16, H * M], BF16))
    bs2_sb = p1rs.enter_context(nc.sbuf_tensor("bs2_sb", [1, H * M], F32))
    gatewT_sb = p1rs.enter_context(
        nc.sbuf_tensor("gatewT_sb", [128, ND * E], BF16))
    lg_sb = p1rs.enter_context(nc.sbuf_tensor("lg_sb", [128, 8], F32))
    r_sm = p1rs.enter_context(nc.sbuf_tensor("r_sm", [128, 8], F32))
    eqi_sb = p1rs.enter_context(nc.sbuf_tensor("eqi_sb", [128, 8], F32))
    tw_tile = p1rs.enter_context(nc.sbuf_tensor("tw_tile", [128, 8], F32))
    twstage = p1rs.enter_context(nc.sbuf_tensor("twstage", [8, 128], F32))
    es_row = p1rs.enter_context(nc.sbuf_tensor("es_row", [128, H * M], BF16))
    clip_sb = p1rs.enter_context(nc.sbuf_tensor("clip_sb", [128, 512], F32))
    s1T_sb = p1rs.enter_context(nc.sbuf_tensor("s1T_sb", [16, SPAN], BF16))
    p1ln = ExitStack()
    x_tile = [p1ln.enter_context(nc.sbuf_tensor(f"x_tile{i}", [128, D], F32))
              for i in range(2)]
    junk_bf = p1ln.enter_context(nc.sbuf_tensor("junk_bf", [128, D], BF16))
    ln_sm = p1ln.enter_context(nc.sbuf_tensor("ln_sm", [128, 16], F32))
    xn_tm = [p1ln.enter_context(nc.sbuf_tensor(f"xn_tm{i}", [128, D], BF16))
             for i in range(2)]

    p1p = ExitStack()
    trp_ps = [p1p.enter_context(nc.psum_tensor(f"trp{i}_ps", [128, 128], BF16))
              for i in range(2)]
    pj_ps = [p1p.enter_context(nc.psum_tensor(f"pj{i}", [128, 512], F32))
             for i in range(4)]
    G = {}

    def grp(key):
        G[key] = Grp()
        return G[key]

    for dt_ in range(ND):
        dma("sync", gatewT_sb[:, dt_ * E:(dt_ + 1) * E],
            gatewT_b[dt_ * 128:(dt_ + 1) * 128, :], dConst,
            waits=[(cGw, cGw.n)])
    for dt_ in range(ND):
        dma("sync", ws1T_sb[:, dt_ * 16:(dt_ + 1) * 16],
            ws1T_b[dt_ * 128:(dt_ + 1) * 128, :], dConst,
            waits=[(cW1, cW1.n)])
    dma("sync", ws2T_sb[:, :], ws2T_b[:, :], dConst, waits=[(cW2s, cW2s.n)])
    dma("sync", bs2_sb[:, :], bs2[:, :], dConst)
    CONST = dConst.n

    xn_done = [None] * NCH
    trT = [None] * NCH
    evprev = None
    for tt in range(NCH):
        par = tt % 2
        xld = dma("sync", x_tile[par][:, :], x_in[tt * 128:(tt + 1) * 128, :],
                  dX[par], waits=[(vS, xn_done[tt - 2])] if tt >= 2 else [])
        pc = 8 * par
        tsq = s_op(lambda e, p=par, c=pc: e.activation(
            junk_bf[:, :], x_tile[p][:, :], AF.Square,
            accum_out=ln_sm[:, c + 4:c + 5]),
            waits=[(dX[par], xld), (vS, xn_done[tt - 2] if tt >= 2 else None)])
        t1 = v_op(lambda e, p=par, c=pc: e.reduce_sum(
            ln_sm[:, c:c + 1], x_tile[p][:, :], axis=AX.X),
            waits=[(dX[par], xld)])
        t2 = v_op(lambda e, c=pc: e.tensor_scalar(
            ln_sm[:, c + 1:c + 2], ln_sm[:, c:c + 1], 1.0 / D, None, ALU.mult),
            waits=[(vS, t1)])
        t3 = v_op(lambda e, c=pc: e.tensor_mul(
            ln_sm[:, c + 2:c + 3], ln_sm[:, c + 1:c + 2], ln_sm[:, c + 1:c + 2]),
            waits=[(vS, t2)])
        t4 = v_op(lambda e, c=pc: e.scalar_tensor_tensor(
            ln_sm[:, c + 3:c + 4], ln_sm[:, c + 4:c + 5], 1.0 / D,
            ln_sm[:, c + 2:c + 3], ALU.mult, ALU.subtract),
            waits=[(vS, t3), (sS, tsq)])
        t5 = s_op(lambda e, c=pc: e.activation(
            ln_sm[:, c + 5:c + 6], ln_sm[:, c + 3:c + 4], AF.Ln,
            bias=eps_sb[:, 0:1]),
            waits=[(vS, t4), (vS, eps_set)])
        t6 = s_op(lambda e, c=pc: e.activation(
            ln_sm[:, c + 6:c + 7], ln_sm[:, c + 5:c + 6], AF.Exp, scale=-0.5),
            waits=[(sS, t5)])
        xn_done[tt] = v_op(lambda e, p=par, c=pc: e.tensor_scalar(
            xn_tm[p][:, :], x_tile[p][:, :],
            ln_sm[:, c + 1:c + 2], ln_sm[:, c + 6:c + 7],
            ALU.subtract, ALU.mult),
            waits=[(sS, t6), (tS, trT[tt - 2] if tt >= 2 else None)])
        ev4 = [None] * ND
        for dt_ in range(ND):
            pp_ = dt_ % 2
            war = ev4[dt_ - 2] if dt_ >= 2 else (
                evprev[dt_ - 2] if evprev is not None else None)
            trt = tr(trp_ps[pp_][:, :],
                     xn_tm[par][:, dt_ * 128:(dt_ + 1) * 128], ident_sb[:, :],
                     waits=[(vS, xn_done[tt]), (vS, war),
                            (dConst, CONST if (tt == 0 and dt_ == 0) else None)])
            ev4[dt_] = v_op(lambda e, d=dt_, i=tt, p2_=pp_: e.tensor_copy(
                xnT_sb[:, d * SPAN + i * 128: d * SPAN + (i + 1) * 128],
                trp_ps[p2_][:, :]),
                waits=[(tS, trt)])
            trT[tt] = trt
        evprev = ev4
    xn_ready = evprev[-1]
    for dt_ in range(ND):
        dma("sync", xnT_s[dt_ * 128:(dt_ + 1) * 128, :],
            xnT_sb[:, dt_ * SPAN:(dt_ + 1) * SPAN], dXn,
            waits=[(vS, evprev[dt_]), (vS, xn_ready if dt_ == ND - 1 else None)])
    xn_scr = dXn.n
    p1ln.close()
    if upto == "p1a":
        return _dbg_finish([("xnT", xnT_s, [(dXn, xn_scr)])])

    p1b = ExitStack()
    prt_ps = p1b.enter_context(nc.psum_tensor("prt_ps", [128, 16], F32))
    ptw_ps = p1b.enter_context(nc.psum_tensor("ptw_ps", [8, 512], F32))
    lg_ev = None
    twtr_ev = None
    for tt in range(NCH):
        g = grp(("rt", tt))
        mmt = None
        for dt_ in range(ND):
            mmt = mm(prt_ps[:, 0:8],
                     xnT_sb[:, dt_ * SPAN + tt * 128: dt_ * SPAN + (tt + 1) * 128],
                     gatewT_sb[:, dt_ * 8:(dt_ + 1) * 8], g, dt_ == ND - 1,
                     waits=[(dConst, CONST), (vS, xn_ready),
                            (vS, lg_ev)] if dt_ == 0 else [])
        lg_ev = v_op(lambda e: e.tensor_copy(lg_sb[:, :], prt_ps[:, 0:8]),
                     waits=[(tS, mmt)])
        a1 = v_op(lambda e: e.reduce_max(r_sm[:, 0:1], lg_sb[:, :], axis=AX.X,
                                         negate=True), waits=[(vS, lg_ev)])
        a2 = v_op(lambda e: e.tensor_scalar(eqi_sb[:, :], lg_sb[:, :],
                                            r_sm[:, 0:1], 0.0, ALU.add,
                                            ALU.is_ge), waits=[(vS, a1)])
        a3 = v_op(lambda e: e.scalar_tensor_tensor(eqi_sb[:, :], eqi_sb[:, :],
                                                   -1e30, lg_sb[:, :],
                                                   ALU.mult, ALU.add),
                  waits=[(vS, a2)])
        a4 = v_op(lambda e: e.reduce_max(r_sm[:, 1:2], eqi_sb[:, :], axis=AX.X,
                                         negate=True), waits=[(vS, a3)])
        a5 = v_op(lambda e: e.tensor_scalar(eqi_sb[:, :], lg_sb[:, :],
                                            r_sm[:, 1:2], 0.0, ALU.add,
                                            ALU.is_ge), waits=[(vS, a4)])
        a6 = s_op(lambda e: e.activation(lg_sb[:, :], lg_sb[:, :], AF.Exp,
                                         bias=r_sm[:, 0:1]),
                  waits=[(vS, a5)])
        a7 = v_op(lambda e: e.tensor_sub(r_sm[:, 2:3], r_sm[:, 1:2],
                                         r_sm[:, 0:1]), waits=[(vS, a4)])
        a7b = v_op(lambda e: e.tensor_scalar(r_sm[:, 3:4], r_sm[:, 2:3], -1.0,
                                             None, ALU.mult), waits=[(vS, a7)])
        a8 = s_op(lambda e: e.activation(r_sm[:, 4:5], r_sm[:, 3:4], AF.Exp),
                  waits=[(vS, a7b), (sS, a6)])
        a9 = v_op(lambda e: e.tensor_scalar(r_sm[:, 5:6], r_sm[:, 4:5], 1.0,
                                            None, ALU.add), waits=[(sS, a8)])
        a10 = v_op(lambda e: e.reciprocal(r_sm[:, 6:7], r_sm[:, 5:6]),
                   waits=[(vS, a9)])
        a11 = v_op(lambda e: e.scalar_tensor_tensor(tw_tile[:, :], lg_sb[:, :],
                                                    r_sm[:, 6:7], eqi_sb[:, :],
                                                    ALU.mult, ALU.mult),
                   waits=[(vS, a10), (sS, a6)])
        twtr = tr(ptw_ps[0:8, 0:128], tw_tile[:, :], identf_sb[:, :],
                  waits=[(vS, a11), (vS, twtr_ev)])
        twtr_ev = v_op(lambda e: e.tensor_copy(
            twstage[0:8, :], ptw_ps[0:8, 0:128]),
            waits=[(tS, twtr), (dTww, 16 * tt if tt > 0 else None)])
        dma("sync", twT_s[0:8, tt * 128:(tt + 1) * 128], twstage[0:8, :],
            dTww, waits=[(vS, twtr_ev)])
    TW_SCR = dTww.n
    p1b.close()
    if upto == "p1b":
        return _dbg_finish([("twT", twT_s, [(dTww, TW_SCR)])])

    ps1_ps = pj_ps[2][0:16, :]
    pzp_ps = [pj_ps[2][0:1, :], pj_ps[3][0:1, :]]
    s1_ev = None
    for tb in range(4):
        g = grp(("s1", tb))
        mmt = None
        for dt_ in range(ND):
            mmt = mm(ps1_ps, ws1T_sb[:, dt_ * 16:(dt_ + 1) * 16],
                     xnT_sb[:, dt_ * SPAN + tb * 512: dt_ * SPAN + (tb + 1) * 512],
                     g, dt_ == ND - 1,
                     waits=[(dConst, CONST), (vS, xn_ready),
                            (vS, s1_ev)] if dt_ == 0 else [])
        s1_ev = v_op(lambda e, b=tb: e.tensor_copy(
            s1T_sb[:, b * 512:(b + 1) * 512], ps1_ps), waits=[(tS, mmt)])
    s1_ready = s1_ev

    es_scr = [None] * NCH
    gz_h = [grp(("zp", i)) for i in range(2)]
    clip_ev = None
    es_ev = None
    zp_ev = None
    colsum_t = None
    for tt in range(NCH):
        for half in range(2):
            g = grp(("sp", tt, half))
            m1 = mm(pj_ps[half][:, :], s1T_sb[0:16, tt * 128:(tt + 1) * 128],
                    ws2T_sb[0:16, half * 512:(half + 1) * 512], g, False,
                    waits=[(vS, s1_ready), (vS, clip_ev)])
            m2 = mm(pj_ps[half][:, :], ones_f_sb[0:1, :],
                    bs2_sb[0:1, half * 512:(half + 1) * 512], g, True)
            clip_ev = v_op(lambda e, h=half: e.tensor_scalar(
                clip_sb[:, :], pj_ps[h][:, :], -CLAMP, CLAMP, ALU.max, ALU.min),
                waits=[(tS, m2), (sS, es_ev)])
            es_ev = s_op(lambda e, h=half: e.activation(
                es_row[:, h * 512:(h + 1) * 512], clip_sb[:, :], AF.Exp),
                waits=[(vS, clip_ev), (tS, colsum_t),
                       (dEs, 16 * tt if tt > 0 else None)])
        for half in range(2):
            colsum_t = mm(pzp_ps[half], onescol_sb[:, 0:1],
                          es_row[:, half * 512:(half + 1) * 512], gz_h[half],
                          True, waits=[(sS, es_ev), (vS, zp_ev)])
        for half in range(2):
            zp_ev = v_op(lambda e, h=half: e.tensor_copy(
                zrow_stage[0:1, h * 512:(h + 1) * 512], pzp_ps[h]),
                waits=[(tS, colsum_t), (dZw, 16 * tt if tt > 0 else None)])
        es_scr[tt] = dma("sync", estm_s[tt, :, :], es_row[:, :], dEs,
                         waits=[(sS, es_ev)])
        dma("sync", zcs_s[tt:tt + 1, :], zrow_stage[0:1, :], dZw,
            waits=[(vS, zp_ev)])
    zfinal_ev = zp_ev
    ES_SCR = dEs.n
    if upto == "p1c":
        return _dbg_finish([("es", estm_s, [(dEs, ES_SCR)]),
                            ("zcs", zcs_s, [(dZw, dZw.n)])])

    p1rs.close()

    p1k = ExitStack()
    wT_sb = p1k.enter_context(nc.sbuf_tensor("wT_sb", [128, ND * D], BF16))
    cos_sb = p1k.enter_context(nc.sbuf_tensor("cos_sb", [64, SPAN], F32))
    sin_sb = p1k.enter_context(nc.sbuf_tensor("sin_sb", [64, SPAN], F32))
    stage_k = [p1k.enter_context(nc.sbuf_tensor(f"stage_k{i}", [128, 512], BF16))
               for i in range(2)]
    rope_t1 = p1k.enter_context(nc.sbuf_tensor("rope_t1", [64, 512], F32))
    rope_t2 = p1k.enter_context(nc.sbuf_tensor("rope_t2", [64, 512], F32))
    p1km = ExitStack()
    ktm_stage = p1km.enter_context(
        nc.sbuf_tensor("ktm_stage", [128, 4 * 2048], BF16))
    tbl_k = [dma("sync", cos_sb[:, :], cosk_in[:, :], dConst,
                 waits=barrier()),
             dma("sync", sin_sb[:, :], sink_in[:, :], dConst)]
    TBLK = dConst.n
    for dt_ in range(ND):
        dma("sync", wT_sb[:, dt_ * D:(dt_ + 1) * D],
            wkT_b[dt_ * 128:(dt_ + 1) * 128, :], dW, waits=[(cWk, cWk.n)])
    k_str = dW.n

    def rope_evict(psum, cos_sb, sin_sb, tb, dst, waits, stage_war):
        c0, c1 = tb * 512, (tb + 1) * 512
        w1 = v_op(lambda e: e.tensor_mul(rope_t1[:, :], psum[0:64, :],
                                         cos_sb[:, c0:c1]), waits=waits)
        w2 = v_op(lambda e: e.tensor_mul(rope_t2[:, :], psum[64:128, :],
                                         sin_sb[:, c0:c1]), waits=[(vS, w1)])
        w3 = v_op(lambda e: e.tensor_sub(dst[0:64, :], rope_t1[:, :],
                                         rope_t2[:, :]),
                  waits=[(vS, w2)] + stage_war)
        w4 = v_op(lambda e: e.tensor_mul(rope_t1[:, :], psum[64:128, :],
                                         cos_sb[:, c0:c1]), waits=[(vS, w3)])
        w5 = v_op(lambda e: e.tensor_mul(rope_t2[:, :], psum[0:64, :],
                                         sin_sb[:, c0:c1]), waits=[(vS, w4)])
        w6 = v_op(lambda e: e.tensor_add(dst[64:128, :], rope_t1[:, :],
                                         rope_t2[:, :]), waits=[(vS, w5)])
        return w6

    kT_dma = {}
    ktm_dma = [None] * NCH
    ktm_ev = {}
    rope_done = [None, None]
    tr_last = {}
    for tb in range(4):
        for h in range(H):
            par = h % 2
            g = grp(("kp", tb, h))
            mmt = None
            for dt_ in range(ND):
                mmt = mm(pj_ps[par][:, :],
                         wT_sb[:, dt_ * D + h * 128: dt_ * D + (h + 1) * 128],
                         xnT_sb[:, dt_ * SPAN + tb * 512:
                                dt_ * SPAN + (tb + 1) * 512], g, dt_ == ND - 1,
                         waits=[(dW, k_str), (vS, xn_ready),
                                (vS, rope_done[par])] if dt_ == 0 else [])
            stage_war = [(dKw[par], kT_dma.get(par)),
                         (tS, tr_last.get(par))]
            rope_done[par] = rope_evict(pj_ps[par], cos_sb, sin_sb, tb,
                                        stage_k[par],
                                        [(tS, mmt), (dConst, TBLK)], stage_war)
            kT_dma[par] = dma(
                "sync",
                kT_s[tb * 4:(tb + 1) * 4, :, h, :].rearrange("c d t -> d c t"),
                stage_k[par][:, :].rearrange("d (c t) -> d c t", c=4),
                dKw[par], waits=[(vS, rope_done[par])])
            for ci in range(4):
                pp_ = ci % 2
                prev_ev = ktm_ev.get((tb, h, ci - 2))
                if prev_ev is None and (h > 0 or tb > 0):
                    key = (tb, h - 1, ci + 2) if h > 0 else (tb - 1, H - 1,
                                                             ci + 2)
                    prev_ev = ktm_ev.get(key)
                trt = tr(trp_ps[pp_][:, :],
                         stage_k[par][:, ci * 128:(ci + 1) * 128],
                         ident_sb[:, :],
                         waits=[(vS, rope_done[par]), (vS, prev_ev)])
                tr_last[par] = trt
                ktm_ev[(tb, h, ci)] = v_op(
                    lambda e, cc=ci, hh=h, p2_=pp_: e.tensor_copy(
                        ktm_stage[:, cc * 2048 + hh * 128:
                                  cc * 2048 + (hh + 1) * 128],
                        trp_ps[p2_][:, :]),
                    waits=[(tS, trt),
                           (dKtm[ci], ktm_dma[(tb - 1) * 4 + ci]
                            if (tb > 0 and h == 0) else None)])
        for ci in range(4):
            c = tb * 4 + ci
            ktm_dma[c] = dma("sync", ktm_s[c, :, :],
                             ktm_stage[:, ci * 2048:(ci + 1) * 2048], dKtm[ci],
                             waits=[(vS, max(ktm_ev[(tb, hh, ci)]
                                             for hh in range(H)))])
    k_mm_done = tS.n
    KT_SCR = {0: kT_dma[0], 1: kT_dma[1]}
    p1km.close()
    if upto == "p1d":
        return _dbg_finish([("kT", kT_s, [(dKw[0], kT_dma[0]),
                                          (dKw[1], kT_dma[1])]),
                            ("ktm", ktm_s, [(dKtm[i], ktm_dma[12 + i])
                                            for i in range(4)][0:1])])

    p1v = ExitStack()
    stage_v = [p1v.enter_context(nc.sbuf_tensor(f"stage_v{i}", [128, 2048],
                                                BF16)) for i in range(2)]
    for dt_ in range(ND):
        dma("sync", wT_sb[:, dt_ * D:(dt_ + 1) * D],
            wvT_b[dt_ * 128:(dt_ + 1) * 128, :], dW,
            waits=[(cWv, cWv.n), (tS, k_mm_done)])
    v_str = dW.n
    vtm_dma = [None] * NCH
    vev = {}
    for tt in range(NCH):
        par = tt % 2
        for ob in range(4):
            g = grp(("vp", tt, ob))
            mmt = None
            for dt_ in range(ND):
                mmt = mm(pj_ps[ob][:, :],
                         xnT_sb[:, dt_ * SPAN + tt * 128:
                                dt_ * SPAN + (tt + 1) * 128],
                         wT_sb[:, dt_ * D + ob * 512: dt_ * D + (ob + 1) * 512],
                         g, dt_ == ND - 1,
                         waits=[(dW, v_str),
                                (vS, vev.get((tt - 1, ob))),
                                (vS, zfinal_ev if (tt == 0 and ob >= 2)
                                 else None)] if dt_ == 0 else [])
            vev[(tt, ob)] = v_op(lambda e, p=par, o=ob: e.tensor_copy(
                stage_v[p][:, o * 512:(o + 1) * 512], pj_ps[o][:, :]),
                waits=[(tS, mmt),
                       (dVw[par], vtm_dma[tt - 2] if (tt >= 2 and ob == 0)
                        else None)] +
                ([(dKtm[i], ktm_dma[12 + i]) for i in range(4)]
                 if (tt == 0 and ob == 0) else []))
        vtm_dma[tt] = dma("sync", vtm_s[tt, :, :], stage_v[par][:, :], dVw[par],
                          waits=[(vS, max(vev[(tt, o)] for o in range(4)))])
    v_mm_done = tS.n
    if upto == "p1e":
        return _dbg_finish([("vtm", vtm_s, [(dVw[0], vtm_dma[NCH - 2]),
                                            (dVw[1], vtm_dma[NCH - 1])])])

    p1d_ctx = ExitStack()
    d_stage = p1d_ctx.enter_context(nc.sbuf_tensor("d_stage", [128, 2048], F32))
    es_chunk = [p1d_ctx.enter_context(
        nc.sbuf_tensor(f"es_chunk{i}", [128, 1024], BF16)) for i in range(2)]
    dchka = [p1d_ctx.enter_context(
        nc.sbuf_tensor(f"dchka{i}", [128, 2048], BF16)) for i in range(2)]
    s1d_mm = None
    gs1a = grp("dS1a")
    gs1b = grp("dS1b")
    lds = {}
    for c in range(NCH):
        par = c % 2
        w0 = [(dKtm[c % 4], ktm_dma[c]), (dEs, ES_SCR)]
        if c == 0:
            w0 += barrier()
        if c >= 2:
            w0.append((tS, s1d_mm))
        lds[("k", c)] = dma("sync", dchka[par][:, :], ktm_s[c, :, :],
                            dDl[par], waits=w0)
        lds[("e", c)] = dma("sync", es_chunk[par][:, :], estm_s[c, :, :],
                            dDl[par])
        for h in range(H):
            gg = gs1a if h < 8 else gs1b
            pj = pj_ps[0] if h < 8 else pj_ps[1]
            s1d_mm = mm(pj[:, (h % 8) * 64:(h % 8 + 1) * 64],
                        dchka[par][:, h * 128:(h + 1) * 128],
                        es_chunk[par][:, h * 64:(h + 1) * 64], gg,
                        c == NCH - 1 and h % 8 == 7,
                        waits=[(dDl[par], lds[("e", c)]),
                               (vS, vev[(NCH - 1, 0)] if c == 0 else None),
                               (vS, vev[(NCH - 1, 1)] if c == 0 else None)]
                        if h == 0 else [])
    dexp1a = v_op(lambda e: e.tensor_copy(d_stage[:, 0:512], pj_ps[0][:, :]),
                  waits=[(tS, s1d_mm)])
    dexp1b = v_op(lambda e: e.tensor_copy(d_stage[:, 512:1024], pj_ps[1][:, :]),
                  waits=[(tS, s1d_mm)])
    dd1 = dma("sync", dIn[0:128, 0:1024], d_stage[:, 0:1024], dDel,
              waits=[(vS, dexp1a), (vS, dexp1b)])
    s2d_mm = None
    gs2 = [grp(("dS2", q)) for q in range(4)]
    for c in range(NCH):
        par = c % 2
        w0 = [(dVw[c % 2], vtm_dma[c]), (tS, s1d_mm if c < 2 else None)]
        if c >= 2:
            w0.append((tS, s2d_mm))
        lds[("v", c)] = dma("sync", dchka[par][:, :], vtm_s[c, :, :],
                            dDl[par], waits=w0)
        lds[("e2", c)] = dma("sync", es_chunk[par][:, :], estm_s[c, :, :],
                             dDl[par])
        for h in range(H):
            q = h // 4
            s2d_mm = mm(pj_ps[q][0:64, (h % 4) * 128:(h % 4 + 1) * 128],
                        es_chunk[par][:, h * 64:(h + 1) * 64],
                        dchka[par][:, h * 128:(h + 1) * 128], gs2[q],
                        c == NCH - 1 and h % 4 == 3,
                        waits=[(dDl[par], lds[("e2", c)]),
                               (vS, dexp1b if c == 0 else None)]
                        if h == 0 else [])
    dexp2 = []
    for q in range(4):
        dexp2.append(v_op(lambda e, qq=q: e.tensor_copy(
            d_stage[0:64, qq * 512:(qq + 1) * 512], pj_ps[qq][0:64, :]),
            waits=[(tS, s2d_mm)]))
    dd2 = dma("sync", dIn[128:192, :], d_stage[0:64, 0:2048], dDel,
              waits=[(vS, max(dexp2))])
    dd3 = dma("sync", dIn[192:193, 0:1024], zrow_stage[0:1, :], dDel,
              waits=[(vS, zfinal_ev)])
    P.op("gpsimd",
         lambda e: e.collective_compute(
             "AllGather", ALU.bypass, replica_groups=[list(range(NCORES))],
             ins=[dIn.ap().opt()], outs=[dOut.ap().opt()]),
         waits=[(dDel, dDel.n)], incs=[(ccS, 1)])
    s2_mm_done = tS.n
    if upto == "p1f":
        return _dbg_finish([("dout", dOut, [(ccS, 1)])])

    tbl_q = [dma("sync", cos_sb[:, :], cosq_in[:, :], dConst,
                 waits=[(vS, rope_done[0]), (vS, rope_done[1])]),
             dma("sync", sin_sb[:, :], sinq_in[:, :], dConst)]
    TBLQ = dConst.n
    for dt_ in range(ND):
        dma("sync", wT_sb[:, dt_ * D:(dt_ + 1) * D],
            wqT_b[dt_ * 128:(dt_ + 1) * 128, :], dW,
            waits=[(cWq, cWq.n), (tS, s2_mm_done)])
    q_str = dW.n
    qT_dma = {}
    rope_done = [None, None]
    for tb in range(4):
        for h in range(H):
            par = h % 2
            g = grp(("qp", tb, h))
            mmt = None
            for dt_ in range(ND):
                mmt = mm(pj_ps[par][:, :],
                         wT_sb[:, dt_ * D + h * 128: dt_ * D + (h + 1) * 128],
                         xnT_sb[:, dt_ * SPAN + tb * 512:
                                dt_ * SPAN + (tb + 1) * 512], g, dt_ == ND - 1,
                         waits=[(dW, q_str),
                                (vS, rope_done[par])] if dt_ == 0 else [])
            stage_war = [(dKw[par], qT_dma.get(par, KT_SCR[par]))]
            rope_done[par] = rope_evict(pj_ps[par], cos_sb, sin_sb, tb,
                                        stage_k[par],
                                        [(tS, mmt), (dConst, TBLQ)], stage_war)
            qT_dma[par] = dma(
                "sync",
                qT_s[tb * 4:(tb + 1) * 4, :, h, :].rearrange("c d t -> d c t"),
                stage_k[par][:, :].rearrange("d (c t) -> d c t", c=4),
                dKw[par], waits=[(vS, rope_done[par])])
    q_mm_done = tS.n
    QKT_SCR = {0: qT_dma[0], 1: qT_dma[1]}
    if upto == "p1g":
        return _dbg_finish([("qT", qT_s, [(dKw[0], qT_dma[0]),
                                          (dKw[1], qT_dma[1])])])

    for dt_ in range(ND):
        dma("sync", wT_sb[:, dt_ * D:(dt_ + 1) * D],
            wgT_b[dt_ * 128:(dt_ + 1) * 128, :], dW,
            waits=[(cWg, cWg.n), (tS, q_mm_done)])
    g_str = dW.n
    sg_dma = [None] * NCH
    gev = {}
    for tt in range(NCH):
        par = tt % 2
        for ob in range(4):
            g = grp(("gp", tt, ob))
            mmt = None
            for dt_ in range(ND):
                mmt = mm(pj_ps[ob][:, :],
                         xnT_sb[:, dt_ * SPAN + tt * 128:
                                dt_ * SPAN + (tt + 1) * 128],
                         wT_sb[:, dt_ * D + ob * 512: dt_ * D + (ob + 1) * 512],
                         g, dt_ == ND - 1,
                         waits=[(dW, g_str),
                                (sS, gev.get((tt - 1, ob)))] if dt_ == 0 else [])
            gev[(tt, ob)] = s_op(lambda e, p=par, o=ob: e.activation(
                stage_v[p][:, o * 512:(o + 1) * 512], pj_ps[o][:, :], AF.Silu),
                waits=[(tS, mmt),
                       (dGw[par], sg_dma[tt - 2] if (tt >= 2 and ob == 0)
                        else None),
                       (dVw[par], vtm_dma[NCH - 2 + par] if (tt < 2 and ob == 0)
                        else None)])
        sg_dma[tt] = dma("sync", sgtm_s[tt, :, :], stage_v[par][:, :], dGw[par],
                         waits=[(sS, max(gev[(tt, o)] for o in range(4)))])
    g_mm_done = tS.n
    SG_SCR = {0: sg_dma[NCH - 2], 1: sg_dma[NCH - 1]}
    if upto == "p1h":
        return _dbg_finish([("sg", sgtm_s, [(dGw[0], sg_dma[NCH - 2]),
                                            (dGw[1], sg_dma[NCH - 1])])])

    p1d_ctx.close()
    p1v.close()
    p1k.close()
    p1p.close()
    p1.close()

    p2 = ExitStack()

    def sb2(name, shape, dt):
        return p2.enter_context(nc.sbuf_tensor(name, shape, dt))

    hw_sb = sb2("hw_sb", [128, E * NFT * 512], BF16)
    xnq_sb = sb2("xnq_sb", [128, ND * 512], BF16)
    gst = [sb2(f"gst{i}", [128, ND * 256], BF16) for i in range(2)]
    ust = [sb2(f"ust{i}", [128, ND * 256], BF16) for i in range(2)]
    dwn_t = [sb2(f"dwnt{i}", [128, 1024], BF16) for i in range(2)]
    twr_sb = sb2("twr_sb", [128, E * 512], BF16)
    tw4_sb = sb2("tw4_sb", [1, E * 512], F32)
    sg_t = sb2("sg_t", [128, 512], F32)
    ut_t = sb2("ut_t", [128, 512], F32)
    ffn_stage = [sb2(f"ffnst{i}", [128, 1024], F32) for i in range(2)]

    P1_V = vS.n
    P1_S = sS.n
    ffn_dma = {}
    tw_mm_last = {}
    down_ev_last = None
    gu_war = {}
    dn_war = {}
    xnq_last_mm = None
    nffn = 0
    for qt in range(4):
        tq0 = qt * 512
        p2q = ExitStack()
        pg_ps = [p2q.enter_context(
            nc.psum_tensor(f"pg{i}_{qt}", [128, 512], F32)) for i in range(2)]
        pu_ps = [p2q.enter_context(
            nc.psum_tensor(f"pu{i}_{qt}", [128, 512], F32)) for i in range(2)]
        ptww_ps = p2q.enter_context(
            nc.psum_tensor(f"ptww_{qt}", [128, 512], F32))
        for dt_ in range(ND):
            dma("sync", xnq_sb[:, dt_ * 512:(dt_ + 1) * 512],
                xnT_s[dt_ * 128:(dt_ + 1) * 128, tq0:tq0 + 512], dXnq,
                waits=(barrier() if (qt == 0 and dt_ == 0) else []) +
                      ([(tS, xnq_last_mm)] if (qt > 0 and dt_ == 0) else []))
        xnq_ld = dXnq.n
        twp = qt % 2
        for e_ in range(E):
            dma("sync", tw4_sb[0:1, e_ * 512:(e_ + 1) * 512],
                twT_s[e_:e_ + 1, tq0:tq0 + 512], dTw[twp],
                waits=[(dTww, TW_SCR), (tS, tw_mm_last.get(twp))]
                if e_ == 0 else [])
        tw_ld = dTw[twp].n
        twev = None
        for e_ in range(E):
            gtw = grp(("twr", qt, e_))
            mmt = mm(ptww_ps[:, :], ones_f_sb[0:1, :],
                     tw4_sb[0:1, e_ * 512:(e_ + 1) * 512], gtw, True,
                     waits=[(dTw[twp], tw_ld if e_ == 0 else None),
                            (vS, twev),
                            (vS, down_ev_last if e_ == 0 else None)])
            tw_mm_last[twp] = mmt
            twev = v_op(lambda e, x=e_: e.tensor_copy(
                twr_sb[:, x * 512:(x + 1) * 512], ptww_ps[:, :]),
                waits=[(tS, mmt)])
        tw_rep_done = twev

        hw_done = {}
        for e_ in range(E):
            for fb in range(6):
                f0 = fb * 256
                fbw = 256 if fb < 5 else 128
                par = (e_ * 6 + fb) % 2
                ld_last = None
                for dt_ in range(ND):
                    ld_last = dma(
                        "sync", gst[par][:, dt_ * 256: dt_ * 256 + fbw],
                        gateT_b[e_ * D + dt_ * 128: e_ * D + (dt_ + 1) * 128,
                                f0:f0 + fbw], dSg[par],
                        waits=[(cGate, cGate.n), (tS, gu_war.get(par))]
                        if dt_ == 0 else [])
                    ld_last = dma(
                        "sync", ust[par][:, dt_ * 256: dt_ * 256 + fbw],
                        upT_b[e_ * D + dt_ * 128: e_ * D + (dt_ + 1) * 128,
                              f0:f0 + fbw], dSg[par],
                        waits=[(cUp, cUp.n)] if dt_ == 0 else [])
                nft_b = fbw // 128
                for ftl in range(nft_b):
                    ft = fb * 2 + ftl
                    pp = ft % 2
                    gg = grp(("ffg", qt, e_, ft))
                    gu = grp(("ffu", qt, e_, ft))
                    mmg = mmu = None
                    for dt_ in range(ND):
                        w0 = []
                        if dt_ == 0:
                            w0 = [(dSg[par], ld_last), (dXnq, xnq_ld),
                                  (sS, hw_done.get(("sg", e_, ft - 2))),
                                  (vS, hw_done.get(("hw", e_, ft - 2))),
                                  (vS, down_ev_last
                                   if (e_ == 0 and ft == 0) else None)]
                            if qt == 0 and e_ == 0 and ft == 0:
                                w0 += [(vS, P1_V), (sS, P1_S)]
                        mmg = mm(pg_ps[pp][:, :],
                                 gst[par][:, dt_ * 256 + ftl * 128:
                                          dt_ * 256 + (ftl + 1) * 128],
                                 xnq_sb[:, dt_ * 512:(dt_ + 1) * 512], gg,
                                 dt_ == ND - 1, waits=w0)
                        mmu = mm(pu_ps[pp][:, :],
                                 ust[par][:, dt_ * 256 + ftl * 128:
                                          dt_ * 256 + (ftl + 1) * 128],
                                 xnq_sb[:, dt_ * 512:(dt_ + 1) * 512], gu,
                                 dt_ == ND - 1)
                    gu_war[par] = mmu
                    xnq_last_mm = mmu
                    st = s_op(lambda e, p=pp: e.activation(
                        sg_t[:, :], pg_ps[p][:, :], AF.Silu),
                        waits=[(tS, mmg),
                               (vS, hw_done.get(("hw", e_, ft - 1)))])
                    hw_done[("sg", e_, ft)] = st
                    vt = v_op(lambda e, p=pp, x=e_: e.tensor_tensor(
                        ut_t[:, :], pu_ps[p][:, :],
                        twr_sb[:, x * 512:(x + 1) * 512], ALU.mult),
                        waits=[(tS, mmu), (vS, tw_rep_done),
                               (vS, hw_done.get(("hw", e_, ft - 1)))])
                    hw_done[("hw", e_, ft)] = v_op(
                        lambda e, x=e_, f=ft: e.tensor_mul(
                            hw_sb[:, (x * NFT + f) * 512:
                                  (x * NFT + f) * 512 + 512],
                            sg_t[:, :], ut_t[:, :]),
                        waits=[(vS, vt), (sS, st)])
        hw_all = max(v for (k, v) in hw_done.items() if k[0] == "hw")
        p2q.close()

        p2d = ExitStack()
        pd_ps = [p2d.enter_context(
            nc.psum_tensor(f"pd{i}_{qt}", [128, 1024], F32)) for i in range(4)]
        for dh in range(2):
            gd = [[grp(("ffd", qt, dh, t_, dc_)) for dc_ in range(2)]
                  for t_ in range(4)]
            dn_mm = None
            for e_ in range(E):
                for ft in range(NFT):
                    par = (e_ * NFT + ft) % 2
                    lw = [(cDown, cDown.n), (tS, dn_war.get(par))]
                    if e_ == 0 and ft == 0:
                        lw.append((vS, hw_all))
                    ldd = dma("sync", dwn_t[par][:, :],
                              downT_b[e_ * FD + ft * 128:
                                      e_ * FD + (ft + 1) * 128,
                                      dh * 1024:(dh + 1) * 1024], dSd[par],
                              waits=lw)
                    for t_ in range(4):
                        for dc in range(2):
                            dn_mm = mm(
                                pd_ps[t_][:, dc * 512:(dc + 1) * 512],
                                hw_sb[:, (e_ * NFT + ft) * 512 + t_ * 128:
                                      (e_ * NFT + ft) * 512 + (t_ + 1) * 128],
                                dwn_t[par][:, dc * 512:(dc + 1) * 512],
                                gd[t_][dc],
                                e_ == E - 1 and ft == NFT - 1,
                                waits=[(dSd[par], ldd),
                                       (vS, down_ev_last
                                        if (e_ == 0 and ft == 0 and t_ == 0)
                                        else None)]
                                if (t_ == 0 and dc == 0) else [])
                    dn_war[par] = dn_mm
            for t_ in range(4):
                fpar = nffn % 2
                de = v_op(lambda e, t2=t_, fp=fpar: e.tensor_copy(
                    ffn_stage[fp][:, :], pd_ps[t2][:, :]),
                    waits=[(tS, dn_mm), (dFo[fpar], ffn_dma.get(fpar))])
                down_ev_last = de
                ffn_dma[fpar] = dma(
                    "sync",
                    ffn_s[tq0 + t_ * 128: tq0 + (t_ + 1) * 128,
                          dh * 1024:(dh + 1) * 1024],
                    ffn_stage[fpar][:, :], dFo[fpar], waits=[(vS, de)])
                nffn += 1
        p2d.close()
    FFN_SCR = (ffn_dma.get(0), ffn_dma.get(1))
    p2.close()
    if upto == "p2":
        return _dbg_finish([("ffn", ffn_s, [(dFo[0], FFN_SCR[0]),
                                            (dFo[1], FFN_SCR[1])])])

    p45c = ExitStack()
    ogT_sb = p45c.enter_context(
        nc.sbuf_tensor("ogT_sb", [128, H * NCH * 128], BF16))
    p4c = ExitStack()
    umask16_sb = p4c.enter_context(
        nc.sbuf_tensor("umask16_sb", [128, H * 128], F32))
    gnw_sb = p4c.enter_context(nc.sbuf_tensor("gnw_sb", [128, H * DV], F32))
    dConst2 = sem("dConst2")
    ALL_SEMS.append(dConst2)
    dma("sync", umask16_sb[:, :], umask16_in[:, :], dConst2, waits=barrier())
    dma("sync", gnw_sb[:, :], gnw_in[:, :], dConst2)
    CONST2 = dConst2.n

    p3 = ExitStack()
    bS1 = [p3.enter_context(nc.sbuf_tensor(f"bS1_{i}", [128, 1024], F32))
           for i in range(2)]
    bS2 = [p3.enter_context(nc.sbuf_tensor(f"bS2_{i}", [64, 2048], F32))
           for i in range(2)]
    bZ = [p3.enter_context(nc.sbuf_tensor(f"bZ_{i}", [1, 1024], F32))
          for i in range(2)]
    accA = [p3.enter_context(nc.sbuf_tensor(f"accA_{i}", [128, 1024], F32))
            for i in range(2)]
    accB = [p3.enter_context(nc.sbuf_tensor(f"accB_{i}", [64, 2048], F32))
            for i in range(2)]
    accC = [p3.enter_context(nc.sbuf_tensor(f"accC_{i}", [1, 1024], F32))
            for i in range(2)]

    zA = v_op(lambda e: e.memset(accA[1][:, :], 0.0))
    zB = v_op(lambda e: e.memset(accB[1][:, :], 0.0))
    zC = v_op(lambda e: e.memset(accC[1][:, :], 0.0))
    lA, lB, lC = zA, zB, zC
    gld = {}
    for r in range(NCORES):
        base = r * DROWS
        pr = r % 2
        wl = ([(ccS, 1)] + (barrier() if r == 0 else [])) if r < 2 else [
            (vS, gld[("c", r - 2)])]
        gld[("a", r)] = dma("sync", bS1[pr][:, :],
                            dOut[base:base + 128, 0:1024], dGat[pr], waits=wl)
        gld[("b", r)] = dma("sync", bS2[pr][:, :],
                            dOut[base + 128:base + 192, :], dGat[pr])
        gld[("z", r)] = dma("sync", bZ[pr][:, :],
                            dOut[base + 192:base + 193, 0:1024], dGat[pr])
        lA = v_op(lambda e, i=r, p=pr: e.scalar_tensor_tensor(
            accA[i % 2][:, :], bS1[p][:, :], coef_sb[:, i:i + 1],
            accA[(i + 1) % 2][:, :], ALU.mult, ALU.add),
            waits=[(vS, lA), (dGat[pr], gld[("z", r)])])
        lB = v_op(lambda e, i=r, p=pr: e.scalar_tensor_tensor(
            accB[i % 2][:, :], bS2[p][:, :], coef_sb[0:64, i:i + 1],
            accB[(i + 1) % 2][:, :], ALU.mult, ALU.add), waits=[(vS, lB)])
        lC = v_op(lambda e, i=r, p=pr: e.scalar_tensor_tensor(
            accC[i % 2][:, :], bZ[p][:, :], coef_sb[0:1, i:i + 1],
            accC[(i + 1) % 2][:, :], ALU.mult, ALU.add), waits=[(vS, lC)])
        gld[("c", r)] = lC
    i1 = v_op(lambda e: e.tensor_copy(S1sb[:, :], accA[(NCORES - 1) % 2][:, :]),
              waits=[(vS, lA)])
    i2 = v_op(lambda e: e.tensor_copy(S2sb[:, :], accB[(NCORES - 1) % 2][:, :]),
              waits=[(vS, lB)])
    i3 = v_op(lambda e: e.tensor_copy(Zin_sb[:, :], accC[(NCORES - 1) % 2][:, :]),
              waits=[(vS, lC)])
    state_init = max(i1, i2, i3)
    if upto == "p3":
        s1d_o = nc.dram_tensor("dbg_S1", [128, H * M], F32,
                               kind="ExternalOutput")
        s2d_o = nc.dram_tensor("dbg_S2", [64, H * DV], F32,
                               kind="ExternalOutput")
        zd_o = nc.dram_tensor("dbg_Zin", [1, H * M], F32,
                              kind="ExternalOutput")
        dsem2 = sem("dbgs2")
        dma("sync", s1d_o.ap(), S1sb[:, :], dsem2, waits=[(vS, state_init)])
        dma("sync", s2d_o.ap(), S2sb[:, :], dsem2)
        dma("sync", zd_o.ap(), Zin_sb[:, :], dsem2)
        P.op("sync", lambda e: e.dma_start(out=dIn[0:1, 0:4],
                                           in_=x_in[0:1, 0:4]),
             [(dsem2, dsem2.n)], [(dsem2, 16)])
        p3.close()
        with nc.Block() as block:
            P.emit(block)
        top.close()
        return nc
    p3.close()

    p4 = ExitStack()

    def sb4(name, shape, dt):
        return p4.enter_context(nc.sbuf_tensor(name, shape, dt))

    qT_ch = [sb4(f"qT_ch{i}", [128, 2048], BF16) for i in range(2)]
    kT_ch = [sb4(f"kT_ch{i}", [128, 2048], BF16) for i in range(2)]
    ktm_ch = [sb4(f"ktm_ch{i}", [128, 2048], BF16) for i in range(2)]
    vtm_ch = [sb4(f"vtm_ch{i}", [128, 2048], BF16) for i in range(2)]
    sgtm_ch1 = sb4("sgtm_ch1", [128, 2048], BF16)
    sgtm_ch = [sgtm_ch1, sgtm_ch1]
    es_ch = [sb4(f"es_ch{i}", [128, 1024], BF16) for i in range(2)]
    zrow_ch1 = sb4("zrow_ch1", [1, 1024], F32)
    zrow_ch = [zrow_ch1, zrow_ch1]
    S1bb = sb4("S1bb", [128, 1024], BF16)
    S2bb = sb4("S2bb", [64, 2048], BF16)
    Zc_sb = sb4("Zc_sb", [1, 1024], F32)
    sgf_sb = sb4("sgf_sb", [128, 2048], F32)
    AT_sb = sb4("AT_sb", [128, 2048], BF16)
    ok_sb = sb4("ok_sb", [128, 1024], F32)
    Zt_sb = sb4("Zt_sb", [128, 1024], F32)
    enz_sb = sb4("enz_sb", [128, 1024], F32)
    okz_sb = sb4("okz_sb", [128, 1024], F32)
    sub_sb = Zt_sb
    mx_sb = sb4("mx_sb", [128, 16], F32)
    sm_sb = sb4("sm_sb", [128, 16], F32)
    rs_sb = sb4("rs_sb", [128, 16], F32)
    wz_sb = sb4("wz_sb", [128, 1024], BF16)
    wzT_sb = sb4("wzT_sb", [64, 2048], BF16)
    esT_sb = sb4("esT_sb", [64, 2048], BF16)
    W2T_sb = sb4("W2T_sb", [128, 2048], BF16)
    msq_sb = sb4("msq_sb", [128, 16], F32)
    lnv_sb = sb4("lnv_sb", [128, 16], F32)
    rstd_sb = sb4("rstd_sb", [128, 16], F32)
    on_sb = sb4("on_sb", [128, 2048], F32)
    tmp2_sb = sb4("tmp2_sb", [128, 1024], F32)
    og_sb = W2T_sb
    eo_alias = None

    p4p = ExitStack()
    big4 = p4p.enter_context(nc.psum_tensor("big4", [128, 2048], F32))
    bq_ps = p4p.enter_context(nc.psum_tensor("bq_ps", [128, 512], F32))
    ptr4 = [p4p.enter_context(nc.psum_tensor(f"ptr4_{i}", [128, 128], BF16))
            for i in range(2)]

    freeT = [None] * NCH
    freeV = [None] * NCH
    pv = {}
    ld4 = {}
    nch_run = NCH
    stage_stop = None
    if upto and upto.startswith("p4s"):
        stage_stop = upto[3:]
        nch_run = 1
    elif upto and upto.startswith("p4c"):
        nch_run = int(upto[3:])
    for c in range(nch_run):
        par = c % 2
        w0 = []
        if c >= 2:
            w0 = [(tS, freeT[c - 2]), (vS, freeV[c - 2])]
        else:
            w0 = (barrier() if c == 0 else []) + [
                  (dKw[0], QKT_SCR[0]), (dKw[1], QKT_SCR[1]),
                  (dGw[0], sg_dma[NCH - 2]),
                  (dGw[1], sg_dma[NCH - 1]),
                  (dKtm[c % 4], ktm_dma[12 + c % 4]),
                  (dVw[0], vtm_dma[NCH - 2]), (dVw[1], vtm_dma[NCH - 1]),
                  (dEs, ES_SCR), (dZw, dZw.n),
                  (vS, state_init)]
        ld4[(c, "qT")] = dma("sync", qT_ch[par][:, :],
                             qT_s[c, :, :, :].rearrange("d h t -> d (h t)"),
                             dP4[par], waits=w0)
        ld4[(c, "kT")] = dma("sync", kT_ch[par][:, :],
                             kT_s[c, :, :, :].rearrange("d h t -> d (h t)"),
                             dP4[par])
        ld4[(c, "ktm")] = dma("sync", ktm_ch[par][:, :], ktm_s[c, :, :],
                              dP4[par])
        ld4[(c, "vtm")] = dma("sync", vtm_ch[par][:, :], vtm_s[c, :, :],
                              dP4[par])
        ld4[(c, "sg")] = dma("sync", sgtm_ch[par][:, :], sgtm_s[c, :, :],
                             dP4[par],
                             waits=[(vS, pv.get("vSgf"))] if c >= 1 else [])
        ld4[(c, "es")] = dma("sync", es_ch[par][:, :], estm_s[c, :, :],
                             dP4[par])
        if c >= 1:
            ld4[(c, "z")] = dma("sync", zrow_ch[par][0:1, :],
                                zcs_s[c - 1:c, :], dP4[par],
                                waits=[(vS, pv.get("vZc"))])
        ld_all = dP4[par].n

        vS1b = v_op(lambda e: e.tensor_copy(S1bb[:, :], S1sb[:, :]),
                    waits=[(vS, state_init if c == 0 else pv["S1upd"]),
                           (tS, pv.get("T2last"))])
        vS2b = v_op(lambda e: e.tensor_copy(S2bb[:, :], S2sb[:, :]),
                    waits=[(vS, state_init if c == 0 else pv["S2upd"]),
                           (tS, pv.get("T7last"))])
        if c == 0:
            vZc = v_op(lambda e: e.tensor_copy(Zc_sb[0:1, :], Zin_sb[0:1, :]),
                       waits=[(vS, state_init)])
        else:
            vZc = v_op(lambda e, p=par: e.tensor_add(
                Zc_sb[0:1, :], Zin_sb[0:1, :], zrow_ch[p][0:1, :]),
                waits=[(dP4[par], ld_all), (tS, pv.get("T3last"))])
        pv["vZc"] = vZc
        vSgf = v_op(lambda e, p=par: e.tensor_copy(sgf_sb[:, :],
                                                   sgtm_ch[p][:, :]),
                    waits=[(dP4[par], ld_all), (vS, pv.get("V16"))])
        pv["vSgf"] = vSgf

        gA = [grp(("A", c, b)) for b in range(4)]
        tA = None
        for h in range(H):
            tA = mm(big4[:, h * 128:(h + 1) * 128],
                    kT_ch[par][:, h * 128:(h + 1) * 128],
                    qT_ch[par][:, h * 128:(h + 1) * 128], gA[h // 4],
                    h % 4 == 3,
                    waits=([(dP4[par], ld_all),
                            (vS, pv.get("V16")), (vS, pv.get("S1upd")),
                            (vS, pv.get("S2upd"))] if h == 0 else []))
        V1 = v_op(lambda e: e.tensor_tensor(AT_sb[:, :], big4[:, :],
                                            umask16_sb[:, :], ALU.mult),
                  waits=[(tS, tA), (tS, pv.get("T2last")),
                         (dConst2, CONST2 if c == 0 else None)])

        okev = []
        ztev = []
        T2last = T3last = None
        prev_read = pv.get("ztev1")
        for half in range(2):
            gok = grp(("ok", c, half))
            gz2 = grp(("Z", c, half))
            for hh in range(8):
                h = half * 8 + hh
                mm(bq_ps[:, hh * 64:(hh + 1) * 64],
                   AT_sb[:, h * 128:(h + 1) * 128],
                   es_ch[par][:, h * 64:(h + 1) * 64], gok, False,
                   waits=([(vS, V1), (dP4[par], ld_all),
                           (vS, prev_read), (vS, vS1b)] if hh == 0 else []))
                T2last = mm(bq_ps[:, hh * 64:(hh + 1) * 64],
                            qT_ch[par][:, h * 128:(h + 1) * 128],
                            S1bb[:, h * 64:(h + 1) * 64], gok, hh == 7)
            okev.append(v_op(lambda e, hf=half: e.tensor_copy(
                ok_sb[:, hf * 512:(hf + 1) * 512], bq_ps[:, :]),
                waits=[(tS, T2last)]))
            for hh in range(8):
                h = half * 8 + hh
                mm(bq_ps[:, hh * 64:(hh + 1) * 64], umaskb_sb[:, :],
                   es_ch[par][:, h * 64:(h + 1) * 64], gz2, False,
                   waits=([(vS, vZc), (vS, okev[half])] if hh == 0 else []))
            T3last = mm(bq_ps[:, :], ones_f_sb[0:1, :],
                        Zc_sb[0:1, half * 512:(half + 1) * 512], gz2, True)
            ztev.append(v_op(lambda e, hf=half: e.tensor_copy(
                Zt_sb[:, hf * 512:(hf + 1) * 512], bq_ps[:, :]),
                waits=[(tS, T3last)]))
            prev_read = ztev[half]
        pv["ztev1"] = ztev[1]
        pv["T2last"] = T2last
        pv["T3last"] = T3last
        if stage_stop == "B":
            pv["S1upd"] = pv["S2upd"] = pv["ogT_done"] = okev[1]
            break

        V3 = v_op(lambda e: e.reciprocal(enz_sb[:, :], Zt_sb[:, :]),
                  waits=[(vS, ztev[1])])
        V4 = v_op(lambda e: e.tensor_mul(okz_sb[:, :], ok_sb[:, :],
                                         enz_sb[:, :]),
                  waits=[(vS, okev[1]), (vS, V3)])
        V5 = v_op(lambda e: e.reduce_max(
            mx_sb[:, :], okz_sb[:, :].rearrange("p (h m) -> p h m", h=H),
            axis=AX.X, negate=True), waits=[(vS, V4)])
        V5b = v_op(lambda e: e.tensor_tensor(
            sub_sb[:, :].rearrange("p (h m) -> p h m", h=H),
            okz_sb[:, :].rearrange("p (h m) -> p h m", h=H),
            mx_sb[:, :, None].broadcast_to([128, H, M]), ALU.add),
            waits=[(vS, V5)])
        S1e = s_op(lambda e: e.activation(ok_sb[:, :], sub_sb[:, :], AF.Exp),
                   waits=[(vS, V5b)])
        V6 = v_op(lambda e: e.reduce_sum(
            sm_sb[:, :], ok_sb[:, :].rearrange("p (h m) -> p h m", h=H),
            axis=AX.X), waits=[(sS, S1e)])
        V7 = v_op(lambda e: e.reciprocal(rs_sb[:, :], sm_sb[:, :]),
                  waits=[(vS, V6)])
        V8 = v_op(lambda e: e.tensor_mul(sub_sb[:, :], ok_sb[:, :],
                                         enz_sb[:, :]), waits=[(sS, S1e)])
        V9 = v_op(lambda e: e.tensor_tensor(
            wz_sb[:, :].rearrange("p (h m) -> p h m", h=H),
            sub_sb[:, :].rearrange("p (h m) -> p h m", h=H),
            rs_sb[:, :, None].broadcast_to([128, H, M]), ALU.mult),
            waits=[(vS, V8), (vS, V7)])

        if stage_stop == "SM":
            pv["S1upd"] = pv["S2upd"] = pv["ogT_done"] = V9
            break
        wzT_ev = {}
        esT_ev = {}
        rot_ev = [pv.get("ogT_done"), pv.get("ogT_done")]
        rot_i = 0
        for h in range(H):
            t1 = tr(ptr4[rot_i % 2][0:64, :], wz_sb[:, h * 64:(h + 1) * 64],
                    ident_sb[:, :],
                    waits=[(vS, V9), (vS, rot_ev[rot_i % 2])])
            wzT_ev[h] = v_op(lambda e, hh=h, p2_=rot_i % 2: e.tensor_copy(
                wzT_sb[0:64, hh * 128:(hh + 1) * 128], ptr4[p2_][0:64, :]),
                waits=[(tS, t1)])
            rot_ev[rot_i % 2] = wzT_ev[h]
            rot_i += 1
            t2 = tr(ptr4[rot_i % 2][0:64, :],
                    es_ch[par][:, h * 64:(h + 1) * 64], ident_sb[:, :],
                    waits=[(vS, rot_ev[rot_i % 2])])
            esT_ev[h] = v_op(lambda e, hh=h, p2_=rot_i % 2: e.tensor_copy(
                esT_sb[0:64, hh * 128:(hh + 1) * 128], ptr4[p2_][0:64, :]),
                waits=[(tS, t2)])
            rot_ev[rot_i % 2] = esT_ev[h]
            rot_i += 1

        if stage_stop == "C":
            pv["S1upd"] = pv["S2upd"] = pv["ogT_done"] = max(
                max(wzT_ev.values()), max(esT_ev.values()))
            break
        gW = [grp(("W2", c, b)) for b in range(4)]
        tW = None
        for h in range(H):
            tW = mm(big4[:, h * 128:(h + 1) * 128],
                    esT_sb[0:64, h * 128:(h + 1) * 128],
                    wzT_sb[0:64, h * 128:(h + 1) * 128], gW[h // 4],
                    h % 4 == 3,
                    waits=[(vS, esT_ev[h]), (vS, wzT_ev[h]),
                           (vS, V1 if h == 0 else None)])
        V12 = v_op(lambda e: e.tensor_tensor(W2T_sb[:, :], big4[:, :],
                                             umask16_sb[:, :], ALU.mult),
                   waits=[(tS, tW)])

        if stage_stop == "D":
            pv["S1upd"] = pv["S2upd"] = pv["ogT_done"] = V12
            break
        gO = [grp(("o", c, b)) for b in range(4)]
        T7last = None
        for h in range(H):
            mm(big4[:, h * 128:(h + 1) * 128],
               W2T_sb[:, h * 128:(h + 1) * 128],
               vtm_ch[par][:, h * 128:(h + 1) * 128], gO[h // 4], False,
               waits=([(vS, V12), (dP4[par], ld_all), (vS, vS2b)]
                      if h == 0 else []))
            T7last = mm(big4[:, h * 128:(h + 1) * 128],
                        wzT_sb[0:64, h * 128:(h + 1) * 128],
                        S2bb[0:64, h * 128:(h + 1) * 128], gO[h // 4],
                        h % 4 == 3)
        pv["T7last"] = T7last
        Sq = s_op(lambda e: e.activation(on_sb[:, :], big4[:, :], AF.Square),
                  waits=[(tS, T7last), (vS, pv.get("V15"))])
        V13 = v_op(lambda e: e.reduce_sum(
            msq_sb[:, :], on_sb[:, :].rearrange("p (h d) -> p h d", h=H),
            axis=AX.X), waits=[(sS, Sq)])
        S2l = s_op(lambda e: e.activation(lnv_sb[:, :], msq_sb[:, :], AF.Ln,
                                          scale=1.0 / DV, bias=eps_sb[:, 0:1]),
                   waits=[(vS, V13)])
        S3e = s_op(lambda e: e.activation(rstd_sb[:, :], lnv_sb[:, :], AF.Exp,
                                          scale=-0.5), waits=[(sS, S2l)])
        V14 = v_op(lambda e: e.tensor_tensor(
            on_sb[:, :].rearrange("p (h d) -> p h d", h=H),
            big4[:, :].rearrange("p (h d) -> p h d", h=H),
            rstd_sb[:, :, None].broadcast_to([128, H, DV]), ALU.mult),
            waits=[(sS, S3e)])
        V16 = None
        for hf in range(2):
            V15 = v_op(lambda e, f=hf: e.tensor_mul(
                tmp2_sb[:, :], on_sb[:, f * 1024:(f + 1) * 1024],
                gnw_sb[:, f * 1024:(f + 1) * 1024]),
                waits=[(vS, V14 if hf == 0 else V16)])
            pv["V15"] = V15
            V16 = v_op(lambda e, f=hf: e.tensor_mul(
                og_sb[:, f * 1024:(f + 1) * 1024], tmp2_sb[:, :],
                sgf_sb[:, f * 1024:(f + 1) * 1024]),
                waits=[(vS, V15), (vS, vSgf if hf == 0 else None)])
        pv["V16"] = V16

        if stage_stop == "E":
            pv["S1upd"] = pv["S2upd"] = pv["ogT_done"] = V16
            break
        ogT_ev = {}
        for h in range(H):
            t3 = tr(ptr4[rot_i % 2][:, :], og_sb[:, h * 128:(h + 1) * 128],
                    ident_sb[:, :],
                    waits=[(vS, V16), (vS, rot_ev[rot_i % 2])])
            ogT_ev[h] = v_op(lambda e, hh=h, p2_=rot_i % 2, cc=c: e.tensor_copy(
                ogT_sb[:, (hh * NCH + cc) * 128:(hh * NCH + cc) * 128 + 128],
                ptr4[p2_][:, :]),
                waits=[(tS, t3)])
            rot_ev[rot_i % 2] = ogT_ev[h]
            rot_i += 1
        pv["ogT_done"] = max(ogT_ev.values())

        gS1u = [grp(("S1u", c, b)) for b in range(2)]
        tS1u = None
        for h in range(H):
            tS1u = mm(big4[:, h * 64:(h + 1) * 64],
                      ktm_ch[par][:, h * 128:(h + 1) * 128],
                      es_ch[par][:, h * 64:(h + 1) * 64], gS1u[h // 8],
                      h % 8 == 7,
                      waits=([(vS, V14), (dP4[par], ld_all)]
                             if h == 0 else []))
        S1upd = v_op(lambda e: e.tensor_add(S1sb[:, :], S1sb[:, :],
                                            big4[:, 0:1024]),
                     waits=[(tS, tS1u), (vS, vS1b)])
        pv["S1upd"] = S1upd
        s2u_ev = []
        tS2u = None
        for half in range(2):
            gS2u = [grp(("S2u", c, half, b)) for b in range(2)]
            for hh in range(8):
                h = half * 8 + hh
                tS2u = mm(big4[0:64, 1024 + hh * 128: 1024 + (hh + 1) * 128],
                          es_ch[par][:, h * 64:(h + 1) * 64],
                          vtm_ch[par][:, h * 128:(h + 1) * 128],
                          gS2u[hh // 4], hh % 4 == 3,
                          waits=([(vS, V14 if half == 0 else s2u_ev[0])]
                                 if hh == 0 else []))
            s2u_ev.append(v_op(lambda e, hf=half: e.tensor_add(
                S2sb[0:64, hf * 1024:(hf + 1) * 1024],
                S2sb[0:64, hf * 1024:(hf + 1) * 1024],
                big4[0:64, 1024:2048]),
                waits=[(tS, tS2u), (vS, vS2b)]))
        pv["S2upd"] = s2u_ev[1]
        freeT[c] = tS2u
        freeV[c] = max(V16, vSgf)
    p4_done = max(pv["S1upd"], pv["S2upd"], pv["ogT_done"])
    if upto and (upto.startswith("p4c") or upto.startswith("p4s")):
        dsem4 = sem("dsem4")
        dumps = [("og", ogT_sb), ("ok", ok_sb), ("zt", Zt_sb),
                 ("eo", ok_sb), ("wz", wz_sb), ("w2t", W2T_sb),
                 ("on", on_sb), ("enz", enz_sb), ("okz", okz_sb),
                 ("mx", mx_sb), ("sm", sm_sb), ("rs", rs_sb),
                 ("s1bb", S1bb), ("s2bb", S2bb), ("zc", Zc_sb),
                 ("esch0", es_ch[0]), ("vtm0", vtm_ch[0]),
                 ("ogsb", og_sb)]
        for nm, t in dumps:
            o = nc.dram_tensor("dbg4_" + nm, list(t.shape), t.dtype,
                               kind="ExternalOutput")
            dma("sync", o.ap(), t[:, :], dsem4, waits=[(vS, p4_done),
                                                       (tS, tS.n),
                                                       (sS, sS.n)])
        P.op("sync", lambda e: e.dma_start(out=dIn[0:1, 0:4],
                                           in_=x_in[0:1, 0:4]),
             [(dsem4, dsem4.n)], [(dsem4, 16)])
        p4p.close()
        p4.close()
        p4c.close()
        with nc.Block() as block:
            P.emit(block)
        top.close()
        return nc
    if upto == "p4":
        og_o = nc.dram_tensor("dbg_og", [128, H * NCH * 128], BF16,
                              kind="ExternalOutput")
        dsem3 = sem("dbgs3")
        dma("sync", og_o.ap(), ogT_sb[:, :], dsem3, waits=[(vS, p4_done)])
        P.op("sync", lambda e: e.dma_start(out=dIn[0:1, 0:4],
                                           in_=x_in[0:1, 0:4]),
             [(dsem3, dsem3.n)], [(dsem3, 16)])
        p4p.close()
        p4.close()
        p4c.close()
        with nc.Block() as block:
            P.emit(block)
        top.close()
        return nc
    p4p.close()
    p4.close()
    p4c.close()

    p5 = ExitStack()
    woT_sb5 = p5.enter_context(nc.sbuf_tensor("woT_sb5", [128, ND * D], BF16))
    x5 = [p5.enter_context(nc.sbuf_tensor(f"x5_{i}", [128, D], F32))
          for i in range(2)]
    f5 = [p5.enter_context(nc.sbuf_tensor(f"f5_{i}", [128, D], F32))
          for i in range(2)]
    yst = [p5.enter_context(nc.sbuf_tensor(f"yst_{i}", [128, D], F32))
           for i in range(2)]
    p5p = ExitStack()
    py_ps = [p5p.enter_context(nc.psum_tensor(f"py{i}", [128, 2048], F32))
             for i in range(2)]
    P4_S = sS.n
    for dt_ in range(ND):
        dma("sync", woT_sb5[:, dt_ * D:(dt_ + 1) * D],
            woT_b[dt_ * 128:(dt_ + 1) * 128, :], dW,
            waits=([(cWo, cWo.n)] + (barrier() if dt_ == 0 else [])))
    wo_ld = dW.n
    yev = {}
    ydma = {}
    for tt in range(NCH):
        par = tt % 2
        xl = dma("sync", x5[par][:, :], x_in[tt * 128:(tt + 1) * 128, :],
                 dP5[par], waits=[(vS, yev.get(tt - 2)),
                                  (dYo[par], ydma.get(tt - 2))])
        fl = dma("sync", f5[par][:, :], ffn_s[tt * 128:(tt + 1) * 128, :],
                 dP5[par],
                 waits=[(dFo[0], FFN_SCR[0]), (dFo[1], FFN_SCR[1])]
                 if tt == 0 else [])
        tmm = None
        for db in range(4):
            gy = grp(("y", tt, db))
            for h in range(H):
                tmm = mm(py_ps[par][:, db * 512:(db + 1) * 512],
                         ogT_sb[:, (h * NCH + tt) * 128:
                                (h * NCH + tt) * 128 + 128],
                         woT_sb5[:, h * D + db * 512: h * D + (db + 1) * 512],
                         gy, h == H - 1,
                         waits=([(dW, wo_ld),
                                 (vS, p4_done if tt == 0 else None),
                                 (sS, P4_S if tt == 0 else None),
                                 (vS, yev.get(tt - 2))]
                                if (db == 0 and h == 0) else []))
        y1 = v_op(lambda e, p=par: e.tensor_add(yst[p][:, :], py_ps[p][:, :],
                                                x5[p][:, :]),
                  waits=[(tS, tmm), (dP5[par], fl)])
        yev[tt] = v_op(lambda e, p=par: e.tensor_add(x5[p][:, :],
                                                     yst[p][:, :],
                                                     f5[p][:, :]),
                       waits=[(vS, y1), (dYo[par], ydma.get(tt - 2))])
        ydma[tt] = dma("sync", y_out[tt * 128:(tt + 1) * 128, :], x5[par][:, :],
                       dYo[par], waits=[(vS, yev[tt])])
    P.op("sync", lambda e: e.dma_start(out=yst[0][0:1, 0:4], in_=x_in[0:1, 0:4]),
         waits=[(dYo[0], dYo[0].n), (dYo[1], dYo[1].n)], incs=[(dP5[0], 16)])
    p5p.close()
    p5.close()

    if fulldbg:
        dsemf = sem("dsemf")
        og_o = nc.dram_tensor("dbg_og", [128, H * NCH * 128], BF16,
                              kind="ExternalOutput")
        ffn_o = nc.dram_tensor("dbg_ffn", [SPAN, D], F32, kind="ExternalOutput")
        dout_o = nc.dram_tensor("dbg_dout", [NCORES * DROWS, 2048], F32,
                                kind="ExternalOutput")
        s1_o = nc.dram_tensor("dbg_S1", [128, H * M], F32, kind="ExternalOutput")
        zin_o = nc.dram_tensor("dbg_Zin", [1, H * M], F32, kind="ExternalOutput")
        dma("gpsimd", og_o.ap(), ogT_sb[:, :], dsemf, waits=[(vS, p4_done)])
        dma("gpsimd", ffn_o.ap(), ffn_s.ap(), dsemf,
            waits=[(dFo[0], FFN_SCR[0]), (dFo[1], FFN_SCR[1])])
        dma("gpsimd", dout_o.ap(), dOut.ap(), dsemf, waits=[(ccS, 1)])
        dma("gpsimd", s1_o.ap(), S1sb[:, :], dsemf, waits=[(vS, state_init)])
        dma("gpsimd", zin_o.ap(), Zin_sb[:, :], dsemf, waits=[(vS, state_init)])
        P.op("gpsimd", lambda e: e.dma_start(out=dIn[0:1, 0:4],
                                             in_=x_in[0:1, 0:4]),
             [(dsemf, dsemf.n)], [(dsemf, 16)])
    p45c.close()
    with nc.Block() as block:
        P.emit(block)
    top.close()
    return nc


_NC_CACHE = {}


def _get_nc():
    if "nc" not in _NC_CACHE:
        _NC_CACHE["nc"] = build_nc()
    return _NC_CACHE["nc"]


def _host_inputs(full):
    x = np.ascontiguousarray(np.asarray(full["L0_attn_hidden_states"],
                                        np.float32).reshape(B * T, D))
    wqT = np.ascontiguousarray(np.asarray(full["L0_attn_Wq"], np.float32).T)
    wkT = np.ascontiguousarray(np.asarray(full["L0_attn_Wk"], np.float32).T)
    wvT = np.ascontiguousarray(np.asarray(full["L0_attn_Wv"], np.float32).T)
    wgT = np.ascontiguousarray(np.asarray(full["L0_attn_Wg"], np.float32).T)
    woT = np.ascontiguousarray(np.asarray(full["L0_attn_Wo"], np.float32).T)
    ws1T = np.ascontiguousarray(np.asarray(full["L0_attn_Ws1"], np.float32).T)
    ws2T = np.ascontiguousarray(np.asarray(full["L0_attn_Ws2"], np.float32).T)
    bs2 = np.ascontiguousarray(np.asarray(full["L0_attn_bs2"],
                                          np.float32).reshape(1, H * M))
    gnw = np.asarray(full["L0_attn_g_norm_weight"], np.float32).reshape(1, DV)
    gnwrep = np.ascontiguousarray(np.tile(gnw, (128, H)))
    gatewT = np.ascontiguousarray(np.asarray(full["L0_ffn_gate_w"],
                                             np.float32).T)
    gateT = np.ascontiguousarray(np.asarray(full["L0_ffn_gate_proj"],
                                            np.float32)
                                 .transpose(0, 2, 1).reshape(E * D, FD))
    upT = np.ascontiguousarray(np.asarray(full["L0_ffn_up_proj"], np.float32)
                               .transpose(0, 2, 1).reshape(E * D, FD))
    downT = np.ascontiguousarray(np.asarray(full["L0_ffn_down_proj"],
                                            np.float32)
                                 .transpose(0, 2, 1).reshape(E * FD, D))

    identb = np.eye(128, dtype=BF)
    identf = np.eye(128, dtype=np.float32)
    umask = np.triu(np.ones((128, 128), np.float32))
    umaskb = umask.astype(BF)
    umask16 = np.ascontiguousarray(np.tile(umask, (1, H)))
    onesf = np.ones((1, 128), np.float32)
    onescol = np.ones((128, 1), BF)

    inv_freq = (1.0 / ROPE_BASE ** (np.arange(0, DK, 2, dtype=np.float64)
                                    / DK))
    maps = []
    for cid in range(NCORES):
        seq, pos = cid // 4, cid % 4
        g0 = seq * T + pos * SPAN
        tpos = np.arange(pos * SPAN, (pos + 1) * SPAN, dtype=np.float64)
        ang = np.outer(inv_freq, tpos)
        cosv = np.cos(ang).astype(np.float32)
        sinv = np.sin(ang).astype(np.float32)
        coef = np.zeros((128, NCORES), np.float32)
        for r in range(NCORES):
            if r // 4 == seq and r % 4 < pos:
                coef[:, r] = 1.0
        maps.append(dict(
            x=np.ascontiguousarray(x[g0:g0 + SPAN]),
            wqT=wqT, wkT=wkT, wvT=wvT, wgT=wgT, woT=woT,
            ws1T=ws1T, ws2T=ws2T, bs2=bs2, gatewT=gatewT,
            gateT=gateT, upT=upT, downT=downT,
            cosq=np.ascontiguousarray(cosv * np.float32(SCALE)),
            sinq=np.ascontiguousarray(sinv * np.float32(SCALE)),
            cosk=cosv, sink=sinv,
            identb=identb, identf=identf, umaskb=umaskb, umask16=umask16,
            onesf=onesf, onescol=onescol, gnwrep=gnwrep, coef=coef,
        ))
    return maps


def kernel(**inputs):
    nc = _get_nc()
    maps = _host_inputs(inputs)
    res = run_bass_kernel_spmd(nc, maps, core_ids=list(range(NCORES)))
    out = np.concatenate([np.asarray(r["y"]) for r in res.results], axis=0)
    return out.reshape(B, T, D)


if __name__ == "__main__":
    print("building...")
    nc = build_nc()
    print("built ok, instrs:", len(nc.m.functions[0].basic_blocks)
          if hasattr(nc.m.functions[0], "basic_blocks") else "?")
